# version 01
# baseline (speedup 1.0000x reference)
import numpy as np
import ml_dtypes
from contextlib import ExitStack

import concourse.bass as bass
import concourse.mybir as mybir
from concourse.bass import IndirectOffsetOnAxis

F32 = mybir.dt.float32
BF16 = mybir.dt.bfloat16
I32 = mybir.dt.int32
AF = mybir.ActivationFunctionType
ALU = mybir.AluOpType
AX = mybir.AxisListType
ET = mybir.EngineType

T = 4096
D = 1024
NH = 16
HD = 64
L = 32
NSTRIPE = 8
STRIPE = 512
NTILE = 32
E = 16
CAP = 2560
LC = 640
NCORE = 8
EPS = 1e-5
BIG = float(1 << 30)

ENGS = ("tensor", "vector", "scalar", "gpsimd", "sync")


CHANNELS = ("tensor", "vector", "scalar", "gpsimd", "sync",
            "scalar_dma", "gpsimd_dma", "sync_dma", "out_dma")


class Ledger:

    def __init__(self, nc):
        self.nc = nc
        self.q = {e: [] for e in ENGS}
        self.count = {c: 0 for c in CHANNELS}
        self.maxwait = {e: {c: 0 for c in CHANNELS} for e in ENGS}
        self.sems = {}
        self.ninst = 0

    def setup_sems(self, ctx):
        for c in CHANNELS:
            self.sems[c] = ctx.enter_context(self.nc.semaphore(f"led_{c}"))

    def emit(self, eng, fn, deps=(), dma=False, out=False, inc=True):
        assert eng in ENGS
        if dma:
            chan = "out_dma" if out else (
                eng + "_dma" if eng != "sync" else "sync_dma")
            amount = 16
        else:
            chan = eng
            amount = 1
        deps = [d for d in deps if d is not None]
        needed = []
        mw = self.maxwait[eng]
        for (dc, dv) in deps:
            if dc == eng and not dma:
                pass
            if dv > mw[dc]:
                needed.append((dc, dv))
        for (dc, dv) in needed:
            mw[dc] = dv
        if inc:
            self.count[chan] += amount
        target = self.count[chan]
        sems = self.sems
        self.ninst += 1 + len(needed)

        def run(engobj, _fn=fn, _needed=tuple(needed), _inc=inc,
                _amt=amount, _chan=chan):
            for (dc, dv) in _needed:
                engobj.wait_ge(sems[dc], dv)
            try:
                inst = _fn(engobj)
            except Exception:
                import traceback, sys
                print("LEDGER FAIL ctx:", getattr(_fn, "dbg", None), file=sys.stderr)
                raise
            if _inc:
                if isinstance(inst, (list, tuple)):
                    inst = inst[-1]
                assert inst is not None, f"emit on {eng} returned no instruction"
                inst.then_inc(sems[_chan], _amt)
            return inst

        self.q[eng].append(run)
        return (chan, target) if inc else None

    def barrier_token(self, chan):
        return (chan, self.count[chan])

    def play(self, block):
        led = self

        @block.tensor
        def _(tensor):
            for fn in led.q["tensor"]:
                fn(tensor)

        @block.vector
        def _(vector):
            for fn in led.q["vector"]:
                fn(vector)

        @block.scalar
        def _(scalar):
            for fn in led.q["scalar"]:
                fn(scalar)

        @block.gpsimd
        def _(gpsimd):
            for fn in led.q["gpsimd"]:
                fn(gpsimd)

        @block.sync
        def _(sync):
            for fn in led.q["sync"]:
                fn(sync)



def host_constants():
    c = {}
    inv_freq = 1.0 / 10000.0 ** (np.arange(0, L, 2, dtype=np.float32) / L)
    t = np.arange(T, dtype=np.float32)
    freqs = np.outer(t, inv_freq)
    cos = np.repeat(np.cos(freqs), 2, axis=-1)[:, :L].astype(np.float32)
    sin = np.repeat(np.sin(freqs), 2, axis=-1)[:, :L].astype(np.float32)
    cosT = cos.T
    sinT = sin.T
    m_cs = np.concatenate([cosT, sinT, cosT, sinT], axis=0)
    c["m_cs"] = m_cs.astype(ml_dtypes.bfloat16)
    c["triu128"] = np.triu(np.ones((128, 128), np.float32)).astype(ml_dtypes.bfloat16)
    c["ident128"] = np.eye(128, dtype=np.float32).astype(ml_dtypes.bfloat16)
    c["iota16"] = np.broadcast_to(np.arange(E, dtype=np.float32), (128, E)).copy()
    c["ones_row"] = np.ones((1, 128), np.float32).astype(ml_dtypes.bfloat16)
    c["ones_row_f32"] = np.ones((1, 128), np.float32)
    c["ones_col"] = np.ones((128, 1), np.float32).astype(ml_dtypes.bfloat16)
    R = np.zeros((L, L), np.float32)
    for i in range(L // 2):
        R[2 * i, 2 * i + 1] = -1.0
        R[2 * i + 1, 2 * i] = 1.0
    c["RT"] = R.T.copy()
    return c


def host_weight_layouts(inp):
    w = {}
    w["cattn_wT"] = np.ascontiguousarray(inp["L0_attn_c_attn_w"].T)
    w["cattn_b"] = np.ascontiguousarray(inp["L0_attn_c_attn_b"].reshape(3072, 1))
    w["q2l_wT"] = np.ascontiguousarray(inp["L0_attn_q2l_w"].T)
    w["q2l_w"] = np.ascontiguousarray(inp["L0_attn_q2l_w"])
    w["q2l_b"] = np.ascontiguousarray(inp["L0_attn_q2l_b"].reshape(L, 1))
    w["kv2l_wT"] = np.ascontiguousarray(inp["L0_attn_kv2l_w"].T)
    w["kv2l_w"] = np.ascontiguousarray(inp["L0_attn_kv2l_w"])
    w["kv2l_b"] = np.ascontiguousarray(inp["L0_attn_kv2l_b"].reshape(L, 1))
    w["wqk_wT"] = np.ascontiguousarray(inp["L0_attn_wqk_w"].T)
    w["wqk_b"] = np.ascontiguousarray(inp["L0_attn_wqk_b"].reshape(L, 1))
    w["l2o_w"] = np.ascontiguousarray(inp["L0_attn_l2o_w"])
    w["l2o_wT"] = np.ascontiguousarray(inp["L0_attn_l2o_w"].T)
    w["l2o_b_row"] = np.ascontiguousarray(inp["L0_attn_l2o_b"].reshape(1, HD))
    w["cprojT"] = np.ascontiguousarray(inp["L0_attn_cproj_w"].T)
    w["cproj_b_row"] = np.ascontiguousarray(inp["L0_attn_cproj_b"].reshape(1, D))
    w["gate_wT"] = np.ascontiguousarray(inp["L0_ffn_gate_w"].T)
    w["w1"] = inp["L0_ffn_w1"]
    w["w2"] = inp["L0_ffn_w2"]
    return w



class PsumSlice:

    def __init__(self, t, c0, w=512):
        self.t, self.c0, self.w = t, c0, w

    def __getitem__(self, idx):
        if isinstance(idx, tuple):
            rows, cols = idx
        else:
            rows, cols = idx, slice(None)
        cs = cols.start if cols.start is not None else 0
        ce = cols.stop if cols.stop is not None else self.w
        return self.t[rows, self.c0 + cs:self.c0 + ce]


class Prog:
    def __init__(self, nc, led, ctx):
        self.nc = nc
        self.led = led
        self.ctx = ctx
        self.sb = {}
        self.ps = {}

    def sbuf(self, name, shape, dtype, scope=None):
        t = (scope or self.ctx).enter_context(self.nc.sbuf_tensor(name, list(shape), dtype))
        self.sb[name] = t
        return t

    def psum(self, name, shape, dtype=F32, scope=None):
        t = (scope or self.ctx).enter_context(self.nc.psum_tensor(name, list(shape), dtype))
        self.ps[name] = t
        return t


def build_setup(p, dram, tmp):
    nc, led = p.nc, p.led
    s = {}

    m_cs = p.sbuf("m_cs", [128, T], BF16)
    triu = p.sbuf("triu", [128, 128], BF16)
    ident = p.sbuf("ident", [128, 128], BF16)
    iota16 = p.sbuf("iota16", [128, E], F32)
    ones_row = p.sbuf("ones_row", [1, 128], BF16)
    ones_row_f32 = p.sbuf("ones_row_f32", [1, 128], F32)
    ones_col = p.sbuf("ones_col", [128, 1], BF16)
    ones_col_f32 = p.sbuf("ones_col_f32", [128, 1], F32)
    wqk4 = p.sbuf("wqk4", [128, L], BF16)
    kv2l2 = p.sbuf("kv2l2", [128, L], BF16)
    gate_wT_sb = p.sbuf("gate_wT_sb", [128, 8 * E], BF16)
    cattn_b_sb = p.sbuf("cattn_b_sb", [128, 24], F32)
    qstackW = p.sbuf("qstackW", [128, HD], BF16)
    kvstackW = p.sbuf("kvstackW", [128, HD], BF16)
    qstackb = p.sbuf("qstackb", [128, 1], F32)
    kvstackb = p.sbuf("kvstackb", [128, 1], F32)
    wqkb4 = p.sbuf("wqkb4", [128, 1], F32)
    C_all = [p.sbuf(f"C_all{g}", [128, D], BF16) for g in range(4)]
    b_all = p.sbuf("b_all", [1, D], BF16)

    q2l_w_sb = p.sbuf("q2l_w_sb", [32, HD], BF16, scope=tmp)
    kv2l_w_sb = p.sbuf("kv2l_w_sb", [32, HD], BF16, scope=tmp)
    RT_sb = p.sbuf("RT_sb", [32, 32], BF16, scope=tmp)
    RT_f32 = p.sbuf("RT_f32", [32, 32], F32, scope=tmp)
    l2o_w2 = p.sbuf("l2o_w2", [128, 32], BF16, scope=tmp)
    l2o_wT_sb = p.sbuf("l2o_wT_sb", [32, HD], BF16, scope=tmp)
    l2o_b_row_sb = p.sbuf("l2o_b_row_sb", [1, HD], BF16, scope=tmp)
    cproj_b_row = p.sbuf("cproj_b_row", [1, D], BF16, scope=tmp)
    kv2l_b_bf = p.sbuf("kv2l_b_bf", [32, 1], BF16, scope=tmp)
    one_cell = p.sbuf("one_cell", [1, 1], BF16, scope=tmp)
    cprojT_sb = p.sbuf("cprojT_sb", [128, 8 * D], BF16, scope=tmp)
    q2l_b_sb = p.sbuf("q2l_b_sb", [32, 1], F32, scope=tmp)
    kv2l_b_sb = p.sbuf("kv2l_b_sb", [32, 1], F32, scope=tmp)
    wqk_b_sb = p.sbuf("wqk_b_sb", [32, 1], F32, scope=tmp)
    b_eff = p.sbuf("b_eff", [128, 1], BF16, scope=tmp)

    for nm in ["m_cs", "triu", "ident", "iota16", "ones_row", "ones_row_f32"]:
        led.emit("sync", lambda e, _n=nm: e.dma_start(out=p.sb[_n][:], in_=dram[_n]), dma=True)

    def cdma(dst_ap, src_ap):
        return led.emit("gpsimd", lambda e: e.dma_start(out=dst_ap, in_=src_ap), dma=True)

    def sdma(dst_ap, src_ap):
        return led.emit("sync", lambda e: e.dma_start(out=dst_ap, in_=src_ap), dma=True)

    for b in range(4):
        cdma(wqk4[32 * b:32 * (b + 1), :], dram["wqk_wT"])
    for b in range(2):
        cdma(kv2l2[64 * b:64 * (b + 1), :], dram["kv2l_wT"])
        cdma(l2o_w2[64 * b:64 * (b + 1), :], dram["l2o_w"])
    cdma(q2l_w_sb[:], dram["q2l_w"])
    cdma(kv2l_w_sb[:], dram["kv2l_w"])
    cdma(RT_sb[:], dram["RT"])
    cdma(l2o_wT_sb[:], dram["l2o_wT"])
    cdma(gate_wT_sb.ap().rearrange("p (c e) -> p c e", c=8),
         dram["gate_wT"].rearrange("(c p) e -> p c e", p=128))
    cdma(l2o_b_row_sb[:], dram["l2o_b_row"])
    cdma(cproj_b_row[:], dram["cproj_b_row"])
    cdma(kv2l_b_bf[:], dram["kv2l_b"])
    sdma(RT_f32[:], dram["RT"])
    sdma(cattn_b_sb[:], dram["cattn_b_pt"])
    sdma(q2l_b_sb[:], dram["q2l_b"])
    sdma(kv2l_b_sb[:], dram["kv2l_b"])
    sdma(wqk_b_sb[:], dram["wqk_b"])
    tmem = led.emit("vector", lambda e: e.memset(one_cell[:], 1.0))
    tmem2 = led.emit("vector", lambda e: e.memset(ones_col[:], 1.0))
    tmem3 = led.emit("vector", lambda e: e.memset(ones_col_f32[:], 1.0))
    DG = led.barrier_token("gpsimd_dma")
    DS = led.barrier_token("sync_dma")

    for b in range(2):
        cdma(qstackW[64 * b:64 * b + 64, 0:32], dram["q2l_wT"])
        cdma(kvstackW[64 * b:64 * b + 64, 0:32], dram["kv2l_wT"])
    DG = led.barrier_token("gpsimd_dma")
    pstk = p.psum("pstk", [128, 32], F32, scope=tmp)
    prev = None
    for (dst, wsb) in ((qstackW, q2l_w_sb), (kvstackW, kv2l_w_sb)):
        t1 = led.emit("tensor", lambda e, _w=wsb: e.matmul(
            pstk[0:64, :], _w[:], RT_sb[:], start=True, stop=True),
            deps=[DG, DS, prev])
        t2 = led.emit("tensor", lambda e, _w=wsb: e.matmul(
            pstk[64:128, :], _w[:], RT_sb[:], start=True, stop=True,
            tile_position=(0, 64)), deps=[t1])
        prev = led.emit("vector", lambda e, _d=dst: e.tensor_copy(_d[:, 32:64], pstk[:]),
                        deps=[t2])
    s["stackW_ready"] = prev

    pb = p.psum("pb", [128, 1], F32, scope=tmp)
    prev = None
    for (dst, bsb) in ((qstackb, q2l_b_sb), (kvstackb, kv2l_b_sb)):
        t1 = led.emit("tensor", lambda e, _b=bsb: e.matmul(
            pb[32:64, :], RT_f32[:], _b[:], start=True, stop=True,
            tile_position=(0, 32)), deps=[DS, prev])
        t2 = led.emit("tensor", lambda e, _b=bsb: e.matmul(
            pb[96:128, :], RT_f32[:], _b[:], start=True, stop=True,
            tile_position=(0, 96)), deps=[t1])
        t3 = led.emit("vector", lambda e, _d=dst: e.tensor_copy(_d[:], pb[:]), deps=[t2])
        t4 = led.emit("vector", lambda e, _d=dst, _b=bsb: e.tensor_copy(_d[0:32, :], _b[:]),
                      deps=[t3])
        prev = led.emit("gpsimd", lambda e, _d=dst, _b=bsb: e.dma_start(
            out=_d[64:96, :], in_=_b[:]), deps=[t3], dma=True)
    s["stackb_ready"] = [prev, led.barrier_token("vector")]

    wb_toks = []
    for bnd in range(4):
        wb_toks.append(led.emit("sync", lambda e, _b=bnd: e.dma_start(
            out=wqkb4[32 * _b:32 * (_b + 1), :], in_=dram["wqk_b"]), dma=True))
    s["wqkb_ready"] = led.emit("vector", lambda e: e.tensor_scalar_mul(
        wqkb4[:], wqkb4[:], float(L) ** -0.5), deps=wb_toks)

    ctoks = []
    for j in range(8):
        ctoks.append(cdma(cprojT_sb[:, D * j:D * (j + 1)],
                          dram["cprojT"][128 * j:128 * (j + 1), :]))
    pc = [p.psum(f"pc{i}", [128, 512], F32, scope=tmp) for i in range(2)]
    prev_ev = [None, None]
    for h in range(NH):
        g, band = h // 4, h % 4
        j0, off = h // 2, 64 * (h % 2)
        for half in range(2):
            pidx = (h * 2 + half) % 2
            tm = led.emit("tensor", lambda e, _j0=j0, _off=off, _half=half, _p=pidx, _band=band: e.matmul(
                pc[_p][32 * _band:32 * (_band + 1), :],
                l2o_w2[_off:_off + HD, :],
                cprojT_sb[_off:_off + HD, D * _j0 + 512 * _half: D * _j0 + 512 * (_half + 1)],
                start=True, stop=True,
                tile_position=(_off, 32 * _band)),
                deps=[DG, ctoks[j0], prev_ev[pidx]])
            prev_ev[pidx] = led.emit("vector", lambda e, _g=g, _band=band, _half=half, _p=pidx: e.tensor_copy(
                C_all[_g][32 * _band:32 * (_band + 1), 512 * _half:512 * (_half + 1)],
                pc[_p][32 * _band:32 * (_band + 1), :]), deps=[tm])
    s["C_ready"] = led.barrier_token("vector")

    pbe = p.psum("pbe", [128, 1], F32, scope=tmp)
    tb = []
    for b in range(2):
        t1 = led.emit("tensor", lambda e, _b=b: e.matmul(
            pbe[64 * _b:64 * _b + 64, :], l2o_wT_sb[:], kv2l_b_bf[:],
            start=True, stop=False, tile_position=(0, 64 * _b)),
            deps=[DG, tmem, s["stackb_ready"][0]])
        t2 = led.emit("tensor", lambda e, _b=b: e.matmul(
            pbe[64 * _b:64 * _b + 64, :], l2o_b_row_sb[:], one_cell[:],
            start=False, stop=True, tile_position=(0, 64 * _b)), deps=[t1])
        tb.append(t2)
    tbe = led.emit("vector", lambda e: e.tensor_copy(b_eff[:], pbe[:]), deps=tb)

    pball = p.psum("pball", [1, 512], F32, scope=tmp)
    prev = tbe
    for half in range(2):
        for h in range(NH):
            j0, off = h // 2, 64 * (h % 2)
            prev = led.emit("tensor", lambda e, _j0=j0, _off=off, _half=half, _h=h: e.matmul(
                pball[:], b_eff[_off:_off + HD, :],
                cprojT_sb[_off:_off + HD, D * _j0 + 512 * _half: D * _j0 + 512 * (_half + 1)],
                start=(_h == 0), stop=False,
                tile_position=(_off, 0)), deps=[prev, tbe])
        prev = led.emit("tensor", lambda e, _half=half: e.matmul(
            pball[:], one_cell[:], cproj_b_row[0:1, 512 * _half:512 * (_half + 1)],
            start=False, stop=True), deps=[prev])
        prev = led.emit("vector", lambda e, _half=half: e.tensor_copy(
            b_all[0:1, 512 * _half:512 * (_half + 1)], pball[:]), deps=[prev])
    s["ball_ready"] = prev
    s.update(dict(m_cs=m_cs, triu=triu, ident=ident, iota16=iota16,
                  ones_row=ones_row, ones_col=ones_col, ones_row_f32=ones_row_f32,
                  ones_col_f32=ones_col_f32,
                  wqk4=wqk4, kv2l2=kv2l2, qstackW=qstackW, kvstackW=kvstackW,
                  qstackb=qstackb, kvstackb=kvstackb, wqkb4=wqkb4,
                  C_all=C_all, b_all=b_all, gate_wT_sb=gate_wT_sb,
                  cattn_b_sb=cattn_b_sb))
    s["done"] = [led.barrier_token(c) for c in CHANNELS]
    return s


def alloc_psums(p):
    PF = p.psum("PF", [128, 3072], F32)
    p.psum("tp0", [128, 512], BF16)
    p.psum("ptv", [128, 512], BF16)


def alloc_attn_bufs(p, pscope=None, trans_scope=None):
    b = {}
    b["qT"] = [p.sbuf(f"qT{g}", [128, T], BF16, scope=pscope) for g in range(4)]
    b["kT"] = [p.sbuf(f"kT{g}", [128, T], BF16, scope=pscope) for g in range(4)]
    b["vaug"] = p.sbuf("vaug", [128, NH * NTILE * 33], BF16, scope=pscope)
    b["PF"] = p.ps["PF"]
    b["P"] = [PsumSlice(b["PF"], 512 * i) for i in range(6)]
    b["tp0"] = p.ps["tp0"]
    b["ptv"] = p.ps["ptv"]
    ts = trans_scope
    b["cs_stage"] = [p.sbuf(f"cs_stage{i}", [128, 768], BF16, scope=ts) for i in range(2)]
    b["qkvT"] = p.sbuf("qkvT", [128, 24 * STRIPE], BF16, scope=ts)
    b["ln1T"] = p.sbuf("ln1T", [128, T], BF16, scope=ts)
    b["xin"] = [p.sbuf(f"xin{i}", [128, D], F32, scope=ts) for i in range(2)]
    b["ln1t"] = [p.sbuf(f"ln1t{i}", [128, D], BF16, scope=ts) for i in range(2)]
    b["bns"] = [p.sbuf(f"bns{i}", [128, 12], F32, scope=ts) for i in range(2)]
    b["mv"] = [p.sbuf(f"mv{i}", [128, 2], F32, scope=ts) for i in range(2)]
    b["lnsc"] = [p.sbuf(f"lnsc{i}", [128, 4], F32, scope=ts) for i in range(2)]
    b["qm"] = [p.sbuf(f"qm{i}", [128, STRIPE], BF16, scope=ts) for i in range(2)]
    b["qr"] = [p.sbuf(f"qr{i}", [128, STRIPE], BF16, scope=ts) for i in range(2)]
    b["vstage"] = p.sbuf("vstage", [64, STRIPE], BF16, scope=ts)
    return b


def emit_ln_tile(p, b, slot, src_tok, out_ap, in_ap=None):
    led = p.led
    xin = b["xin"][slot] if in_ap is None else in_ap
    bns, mv, lnsc = b["bns"][slot], b["mv"][slot], b["lnsc"][slot]
    src_tok = src_tok if isinstance(src_tok, list) else [src_tok]
    t1a = led.emit("vector", lambda e: e.bn_stats(bns[:, 0:6], xin[:, 0:512]), deps=src_tok)
    t1b = led.emit("vector", lambda e: e.bn_stats(bns[:, 6:12], xin[:, 512:1024]), deps=src_tok)
    t2 = led.emit("vector", lambda e: e.bn_aggr(mv[:], bns[:]), deps=[t1a, t1b])
    t3 = led.emit("vector", lambda e: e.tensor_scalar_add(lnsc[:, 0:1], mv[:, 1:2], EPS), deps=[t2])
    t4 = led.emit("scalar", lambda e: e.activation(lnsc[:, 1:2], lnsc[:, 0:1], AF.Ln), deps=[t3])
    t5 = led.emit("scalar", lambda e: e.activation(lnsc[:, 2:3], lnsc[:, 1:2], AF.Exp, scale=-0.5), deps=[t4])
    t6 = led.emit("vector", lambda e: e.scalar_tensor_tensor(
        lnsc[:, 3:4], mv[:, 0:1], -1.0, lnsc[:, 2:3], ALU.mult, ALU.mult), deps=[t5])
    t7 = led.emit("scalar", lambda e: e.activation(
        out_ap, xin[:], AF.Identity, bias=lnsc[:, 3:4], scale=lnsc[:, 2:3]),
        deps=[t6] + src_tok)
    return t7


def build_stage_a(p, s, b, x_ap, war_deps=()):
    nc, led = p.nc, p.led
    qT, kT, vaug = b["qT"], b["kT"], b["vaug"]
    qkvT, ln1T = b["qkvT"], b["ln1T"]
    ident = s["ident"]
    m_cs = s["m_cs"]

    if not hasattr(p, "_vones"):
        vv0 = vaug.rearrange("p (h k c) -> p h k c", h=NH, k=NTILE)
        p._vones = led.emit("vector", lambda e: e.memset(vv0[:, :, :, 32:33], 1.0),
                            deps=list(war_deps))

    st = {}
    xin_free = {}
    ln_free = {}
    qkv_evac = {0: None, 1: None}
    last_latent_use_of_qkvT = war_list = list(war_deps)
    last_latent = None

    for sidx in range(NSTRIPE):
        stripe_tr = []
        for t4 in range(4):
            g = 4 * sidx + t4
            slot = g % 2
            tok_x = led.emit("sync", lambda e, _g=g, _s=slot: e.dma_start(
                out=b["xin"][_s][:], in_=x_ap[128 * _g:128 * (_g + 1), :]),
                deps=[xin_free.get(slot)] + war_list, dma=True)
            tok_ln = emit_ln_tile(p, b, slot, tok_x, b["ln1t"][slot][:])
            xin_free[slot] = tok_ln
            lnv = ln1T.rearrange("p (j c) -> p j c", j=8)
            for jh in range(2):
                tts = []
                for j4 in range(4):
                    j = 4 * jh + j4
                    tts.append(led.emit("tensor", lambda e, _j=j, _j4=j4, _s=slot: e.transpose(
                        b["tp0"][:, 128 * _j4:128 * (_j4 + 1)],
                        b["ln1t"][_s][:, 128 * _j:128 * (_j + 1)],
                        ident[:]), deps=[tok_ln, st.get("tp0_free"), last_latent]))
                ev = led.emit("vector", lambda e, _jh=jh, _t4=t4: e.tensor_copy(
                    lnv[:, 4 * _jh:4 * (_jh + 1), 128 * _t4:128 * (_t4 + 1)],
                    b["tp0"].ap().rearrange("p (j4 c) -> p j4 c", j4=4)),
                    deps=tts)
                st["tp0_free"] = ev
                stripe_tr.append(ev)
            ln_free[slot] = stripe_tr[-1]

        qkv_toks = []
        for og in range(4):
            mm_last = [None] * 6
            for j in range(8):
                slot = (og * 8 + j) % 2
                stg = b["cs_stage"][slot]
                tok_w = led.emit("gpsimd", lambda e, _j=j, _og=og, _s=slot: e.dma_start(
                    out=b["cs_stage"][_s][:],
                    in_=p.dram["cattn_wT"][128 * _j:128 * (_j + 1),
                                           768 * _og:768 * (_og + 1)]),
                    deps=[st.get(("cs_free", slot))], dma=True)
                for k6 in range(6):
                    of = 6 * og + k6
                    ps = b["P"][k6]
                    dep = [tok_w]
                    if j == 0:
                        dep += stripe_tr + [st.get(("qkvP_free", k6))]
                    mm_last[k6] = led.emit("tensor", lambda e, _j=j, _k=k6, _ps=ps, _s=slot: e.matmul(
                        _ps[:], b["cs_stage"][_s][:, 128 * _k:128 * (_k + 1)],
                        ln1T[:, 512 * _j:512 * (_j + 1)],
                        start=(_j == 0), stop=(_j == 7)), deps=dep)
                st[("cs_free", slot)] = mm_last[5]
            for k6 in range(6):
                of = 6 * og + k6
                tok_e = led.emit("scalar", lambda e, _of=of, _k=k6: e.activation(
                    qkvT[:, 512 * _of:512 * (_of + 1)], b["P"][_k][:], AF.Identity,
                    bias=s["cattn_b_sb"][:, _of:_of + 1], scale=1.0),
                    deps=[mm_last[k6], last_latent])
                st[("qkvP_free", k6)] = tok_e
                qkv_toks.append(tok_e)
        qkv_evac[0] = qkv_toks[-2]
        qkv_evac[1] = qkv_toks[-1]

        lat_reads = []
        for hp in range(8):
            base2 = 64 * (hp % 2)
            colq, colk, colv = 512 * hp, 512 * (hp + 8), 512 * (hp + 16)
            pstk = b["P"][hp % 2]
            pwq = b["P"][2]
            pwk = b["P"][3]
            for (side, stW, stB, col) in (("q", s["qstackW"], s["qstackb"], colq),
                                          ("k", s["kvstackW"], s["kvstackb"], colk)):
                mA = led.emit("tensor", lambda e, _c=col, _w=stW, _ps=pstk: e.matmul(
                    _ps[0:64, :], _w[0:64, :], qkvT[0:64, _c:_c + 512],
                    start=True, stop=True),
                    deps=qkv_toks + [st.get(("st_free", hp % 2)), s["stackW_ready"]])
                mB = led.emit("tensor", lambda e, _c=col, _w=stW, _ps=pstk: e.matmul(
                    _ps[64:128, :], _w[64:128, :], qkvT[64:128, _c:_c + 512],
                    start=True, stop=True, tile_position=(64, 64)), deps=[mA])
                lat_reads += [mA, mB]
                qm = b["qm"][hp % 2]
                ev = led.emit("scalar", lambda e, _ps=pstk, _qm=qm, _sb=stB: e.activation(
                    _qm[:], _ps[:], AF.Identity, bias=_sb[:], scale=1.0),
                    deps=[mB, s["stackb_ready"][0], s["stackb_ready"][1],
                          st.get(("qm_free", hp % 2))])
                st[("st_free", hp % 2)] = ev
                qr = b["qr"][hp % 2]
                rp = led.emit("vector", lambda e, _qm=qm, _qr=qr, _si=sidx: e.tensor_tensor(
                    _qr[:], _qm[:], m_cs[:, 512 * _si:512 * (_si + 1)], ALU.mult),
                    deps=[ev, st.get(("qr_free", hp % 2))])
                st[("qm_free", hp % 2)] = rp
                pw = pwq if side == "q" else pwk
                m2s = []
                for hh in range(2):
                    h = 2 * hp + hh
                    gb = h % 4
                    if side == "q":
                        l1 = s["wqk4"][64 * hh:64 * hh + 32, :]
                        l2 = s["wqk4"][64 * hh + 32:64 * hh + 64, :]
                    else:
                        l1 = ident[64 * hh:64 * hh + 32, 64 * hh:64 * hh + 32]
                        l2 = ident[64 * hh + 32:64 * hh + 64, 64 * hh + 32:64 * hh + 64]
                    m1 = led.emit("tensor", lambda e, _hh=hh, _gb=gb, _ps=pw, _l=l1, _qr=qr: e.matmul(
                        _ps[32 * _gb:32 * (_gb + 1), :], _l,
                        _qr[64 * _hh:64 * _hh + 32, :],
                        start=True, stop=False,
                        tile_position=(64 * _hh, 32 * _gb)),
                        deps=[rp, st.get(("wv_free", side, hp % 2))])
                    m2 = led.emit("tensor", lambda e, _hh=hh, _gb=gb, _ps=pw, _l=l2, _qr=qr: e.matmul(
                        _ps[32 * _gb:32 * (_gb + 1), :], _l,
                        _qr[64 * _hh + 32:64 * _hh + 64, :],
                        start=False, stop=True,
                        tile_position=(64 * _hh + 32, 32 * _gb)),
                        deps=[m1])
                    m2s.append(m2)
                st[("qr_free", hp % 2)] = m2s[-1]
                dst = (qT if side == "q" else kT)[(2 * hp) // 4]
                if side == "q":
                    evq = led.emit("scalar", lambda e, _ps=pw, _d=dst, _b=base2, _si=sidx: e.activation(
                        _d[_b:_b + 64, 512 * _si:512 * (_si + 1)],
                        _ps[_b:_b + 64, :], AF.Identity,
                        bias=s["wqkb4"][_b:_b + 64, :], scale=float(L) ** -0.5),
                        deps=m2s)
                else:
                    evq = led.emit("scalar", lambda e, _ps=pw, _d=dst, _b=base2, _si=sidx: e.activation(
                        _d[_b:_b + 64, 512 * _si:512 * (_si + 1)],
                        _ps[_b:_b + 64, :], AF.Copy),
                        deps=m2s)
                st[("wv_free", side, hp % 2)] = evq
            mvs = []
            for hh in range(2):
                mvs.append(led.emit("tensor", lambda e, _hh=hh, _ps=pstk, _c=colv: e.matmul(
                    _ps[32 * _hh:32 * (_hh + 1), :],
                    s["kv2l2"][64 * _hh:64 * (_hh + 1), :],
                    qkvT[64 * _hh:64 * (_hh + 1), _c:_c + 512],
                    start=True, stop=True, tile_position=(64 * _hh, 32 * _hh)),
                    deps=[qkv_toks[-1], st.get(("st_free", hp % 2)), st.get("vstage_free")]))
            lat_reads += mvs
            evV = led.emit("vector", lambda e, _ps=pstk: e.tensor_copy(
                b["vstage"][:], _ps[0:64, :]), deps=mvs + [st.get("vstage_free")])
            st[("st_free", hp % 2)] = evV
            vv = vaug.rearrange("p (hh k c) -> p hh k c", hh=NH, k=NTILE)
            for hh in range(2):
                h = 2 * hp + hh
                tts = []
                for t4 in range(4):
                    tts.append(led.emit("tensor", lambda e, _t4=t4, _hh=hh: e.transpose(
                        b["ptv"][:, 32 * _t4:32 * (_t4 + 1)],
                        b["vstage"][32 * _hh:32 * (_hh + 1), 128 * _t4:128 * (_t4 + 1)],
                        ident[32 * _hh:32 * (_hh + 1), 32 * _hh:32 * (_hh + 1)]),
                        deps=[evV, st.get("ptv_free")]))
                evT = led.emit("vector", lambda e, _h=h, _si=sidx: e.tensor_copy(
                    vv[:, _h, 4 * _si:4 * (_si + 1), 0:32],
                    b["ptv"][:, 0:128].rearrange("p (k c) -> p k c", k=4)), deps=tts)
                st["ptv_free"] = evT
            st["vstage_free"] = st["ptv_free"]
        last_latent = lat_reads[-1]
    b["stageA_done"] = [led.barrier_token("vector"), led.barrier_token("scalar")]
    return b


def alloc_c_bufs(p, pscope=None, trans_scope=None):
    c = {}
    ts = trans_scope
    c["ynT"] = [p.sbuf(f"ynT{g}", [128, T], BF16, scope=pscope) for g in range(4)]
    c["pt"] = [p.sbuf(f"pt{i}", [128, 2048], BF16, scope=ts) for i in range(2)]
    c["yraw"] = p.sbuf("yraw", [32, 512], F32, scope=ts)
    c["ynstage"] = p.sbuf("ynstage", [32, 512], BF16, scope=ts)
    c["rrow"] = p.sbuf("rrow", [1, 512], F32, scope=ts)
    c["absq"] = p.sbuf("absq", [128, 2], F32, scope=ts)
    c["absk"] = p.sbuf("absk", [128, 1], F32, scope=ts)
    c["crow"] = p.sbuf("crow", [1, E], F32, scope=ts)
    c["crow_bf"] = p.sbuf("crow_bf", [1, E], BF16, scope=ts)
    c["bias_h"] = p.sbuf("bias_h", [128, E], F32, scope=ts)
    return c


def build_stage_c(p, s, a, c, deps_a):
    nc, led = p.nc, p.led
    qT, kT, vaug = a["qT"], a["kT"], a["vaug"]
    PF = a["PF"]
    Sg = PsumSlice(PF, 0, 2048)
    Pya = PsumSlice(PF, 2048, 512)
    Pbc = PsumSlice(PF, 2560, 512)
    triu = s["triu"]
    ident = s["ident"]
    st = {}

    sum_mms = []
    for g in range(4):
        t1 = led.emit("vector", lambda e, _g=g: e.reduce_max(
            c["absq"][:, 0:1], qT[_g][:], AX.X, apply_absolute_value=True),
            deps=deps_a + [st.get("absq_free")])
        t2 = led.emit("vector", lambda e, _g=g: e.reduce_max(
            c["absk"][:], kT[_g][:], AX.X, apply_absolute_value=True),
            deps=[t1])
        t3 = led.emit("vector", lambda e: e.tensor_tensor(
            c["absq"][:, 1:2], c["absq"][:, 0:1], c["absk"][:], ALU.mult),
            deps=[t2])
        for band in range(4):
            h = 4 * g + band
            tm = led.emit("tensor", lambda e, _b=band, _h=h: e.matmul(
                Pbc[0:1, _h:_h + 1],
                c["absq"][32 * _b:32 * (_b + 1), 1:2],
                s["ones_col_f32"][32 * _b:32 * (_b + 1), :],
                start=True, stop=True,
                tile_position=(32 * _b, 0)), deps=[t3, st.get("pbc_free")])
            sum_mms.append(tm)
        st["absq_free"] = sum_mms[-1]
    ev = led.emit("vector", lambda e: e.tensor_copy(c["crow"][:], Pbc[0:1, 0:E]),
                  deps=sum_mms)
    t4 = led.emit("vector", lambda e: e.tensor_scalar(
        c["crow"][:], c["crow"][:], -1.02, -0.05, ALU.mult, ALU.add), deps=[ev])
    t5 = led.emit("vector", lambda e: e.tensor_copy(c["crow_bf"][:], c["crow"][:]),
                  deps=[t4])
    st["pbc_free"] = t5
    bias_done = []
    for h in range(NH):
        tm = led.emit("tensor", lambda e, _h=h: e.matmul(
            Pbc[:, 0:1], s["ones_row"][:], c["crow_bf"][0:1, _h:_h + 1],
            start=True, stop=True), deps=[t5, st.get("pbc_free")])
        tb = led.emit("vector", lambda e, _h=h: e.tensor_copy(
            c["bias_h"][:, _h:_h + 1], Pbc[:, 0:1]), deps=[tm])
        st["pbc_free"] = tb
        bias_done.append(tb)
    bias_all = bias_done[-1]

    for h in range(NH):
        g, band = h // 4, h % 4
        lb = 32 * band
        for Q in range(NSTRIPE):
            n_kt = 4 * (Q + 1)
            last_av = None
            for gi in range(Q + 1):
                diag = (gi == Q)
                slot = gi % 2
                smms = []
                for k4 in range(4):
                    kt = 4 * gi + k4
                    off = 128 * k4 if diag else 0
                    N = 512 - off
                    dep = [st.get("Sg_free"), bias_all] if k4 == 0 else []
                    smms.append(led.emit("tensor", lambda e, _kt=kt, _k4=k4, _off=off, _N=N, _h=h, _g=g, _lb=lb, _Q=Q: e.matmul(
                        Sg[:, 512 * _k4 + _off:512 * _k4 + _off + _N],
                        kT[_g][_lb:_lb + 32, 128 * _kt:128 * (_kt + 1)],
                        qT[_g][_lb:_lb + 32, 512 * _Q + _off:512 * _Q + _off + _N],
                        start=True, stop=True,
                        tile_position=(_lb, 0)), deps=dep))
                texp = led.emit("scalar", lambda e, _s=slot, _h=h: e.activation(
                    c["pt"][_s][:], Sg[:], AF.Exp, bias=c["bias_h"][:, _h:_h + 1]),
                    deps=smms + [st.get(("pt_free", slot))])
                st["Sg_free"] = texp
                avdep = [texp]
                if diag:
                    fixes = []
                    for k4 in range(1, 4):
                        fixes.append(led.emit("vector", lambda e, _s=slot, _k4=k4: e.memset(
                            c["pt"][_s][:, 512 * _k4:512 * _k4 + 128 * _k4], 0.0),
                            deps=[texp]))
                    for k4 in range(4):
                        fixes.append(led.emit("vector", lambda e, _s=slot, _k4=k4: e.tensor_tensor(
                            c["pt"][_s][:, 512 * _k4 + 128 * _k4:512 * _k4 + 128 * (_k4 + 1)],
                            c["pt"][_s][:, 512 * _k4 + 128 * _k4:512 * _k4 + 128 * (_k4 + 1)],
                            triu[:], ALU.mult), deps=[texp]))
                    avdep += fixes
                for k4 in range(4):
                    kt = 4 * gi + k4
                    dep = avdep + ([st.get("yaug_free")] if kt == 0 else [])
                    last_av = led.emit("tensor", lambda e, _kt=kt, _k4=k4, _s=slot, _h=h, _n=n_kt: e.matmul(
                        Pya[0:33, :],
                        vaug[:, (_h * NTILE + _kt) * 33:(_h * NTILE + _kt) * 33 + 33],
                        c["pt"][_s][:, 512 * _k4:512 * (_k4 + 1)],
                        start=(_kt == 0), stop=(_kt == _n - 1)), deps=dep)
                    st[("pt_free", slot)] = last_av
            ty = led.emit("vector", lambda e: e.tensor_copy(c["yraw"][:], Pya[0:32, :]),
                          deps=[last_av, st.get("yraw_free")])
            tr = led.emit("vector", lambda e: e.reciprocal(c["rrow"][:], Pya[32:33, :]),
                          deps=[last_av, st.get("rrow_free")])
            st["yaug_free"] = tr
            trb = led.emit("tensor", lambda e: e.matmul(
                Pbc[0:32, :], s["ones_row_f32"][0:1, 0:32], c["rrow"][:],
                start=True, stop=True), deps=[tr, st.get("pbc_free")])
            tyn = led.emit("vector", lambda e: e.tensor_tensor(
                c["ynstage"][:], c["yraw"][:], Pbc[0:32, :], ALU.mult),
                deps=[trb, ty, st.get("ynstage_free")])
            st["yraw_free"] = tyn
            st["rrow_free"] = trb
            st["pbc_free"] = tyn
            tsh = led.emit("tensor", lambda e, _lb=lb: e.matmul(
                Pya[_lb:_lb + 32, :] if _lb else Pya[0:32, :],
                ident[0:32, 0:32], c["ynstage"][:],
                start=True, stop=True,
                tile_position=(0, _lb)), deps=[tyn, st.get("yaug_free"), st.get("shift_free")])
            st["ynstage_free"] = tsh
            tev = led.emit("vector", lambda e, _g=g, _lb=lb, _Q=Q: e.tensor_copy(
                c["ynT"][_g][_lb:_lb + 32, 512 * _Q:512 * (_Q + 1)],
                Pya[_lb:_lb + 32, :] if _lb else Pya[0:32, :]), deps=[tsh])
            st["shift_free"] = tev
            st["yaug_free"] = tev
    c["stageC_done"] = [led.barrier_token("vector")]
    return c


def alloc_d_persist(p):
    d = {}
    d["onehot_all"] = p.sbuf("onehot_all", [128, NTILE * E], F32)
    d["rank_all"] = p.sbuf("rank_all", [128, NTILE], F32)
    d["eidx_all"] = p.sbuf("eidx_all", [128, NTILE], F32)
    d["ptop_all"] = p.sbuf("ptop_all", [128, NTILE], F32)
    d["cntrow"] = p.sbuf("cntrow", [1, E], F32)
    return d


def alloc_d_bufs(p, d, trans_scope=None):
    ts = trans_scope
    d["xd"] = [p.sbuf(f"xd{i}", [128, D], F32, scope=ts) for i in range(2)]
    d["x1t"] = [p.sbuf(f"x1t{i}", [128, D], F32, scope=ts) for i in range(2)]
    d["ln2t"] = [p.sbuf(f"ln2t{i}", [128, D], BF16, scope=ts) for i in range(2)]
    d["ln2T"] = p.sbuf("ln2T", [128, D], BF16, scope=ts)
    d["bns"] = [p.sbuf(f"dbns{i}", [128, 12], F32, scope=ts) for i in range(2)]
    d["mv"] = [p.sbuf(f"dmv{i}", [128, 2], F32, scope=ts) for i in range(2)]
    d["lnsc"] = [p.sbuf(f"dlnsc{i}", [128, 4], F32, scope=ts) for i in range(2)]
    d["lgf"] = p.sbuf("lgf", [128, E], F32, scope=ts)
    d["lmax"] = p.sbuf("lmax", [128, 2], F32, scope=ts)
    d["scr16"] = p.sbuf("scr16", [128, E], F32, scope=ts)
    d["sume"] = p.sbuf("sume", [128, 1], F32, scope=ts)
    d["oh"] = p.sbuf("oh", [128, E], F32, scope=ts)
    d["oh_bf"] = p.sbuf("oh_bf", [128, E], BF16, scope=ts)
    d["tmp16"] = p.sbuf("tmp16", [128, E], F32, scope=ts)
    d["prefrow"] = p.sbuf("prefrow", [1, E], F32, scope=ts)
    return d


def build_stage_d(p, s, cb, d, x_ap, x1_out, ln2_out, deps_c, upto="full"):
    nc, led = p.nc, p.led
    PF = p.ps["PF"]
    Pat = [PsumSlice(PF, 0), PsumSlice(PF, 512)]
    Ploc = PsumSlice(PF, 1024)
    Pcnt = PsumSlice(PF, 1536)
    Plg = PsumSlice(PF, 2048)
    tp0 = p.ps["tp0"]
    ynT, C_all = cb["ynT"], s["C_all"]
    st = {}

    tz = led.emit("vector", lambda e: e.memset(d["prefrow"][:], 0.0), deps=deps_c)
    st["pref_ready"] = tz
    x1_toks = []
    ln2_toks = []

    for t in range(NTILE):
        slot = t % 2
        at_toks = []
        for half in range(2):
            ps = Pat[half]
            mms = []
            for g in range(4):
                dep = (deps_c + [st.get(("at_free", half))]) if g == 0 else []
                mms.append(led.emit("tensor", lambda e, _g=g, _t=t, _half=half, _ps=ps: e.matmul(
                    _ps[:], ynT[_g][:, 128 * _t:128 * (_t + 1)],
                    C_all[_g][:, 512 * _half:512 * (_half + 1)],
                    start=(_g == 0), stop=False), deps=dep))
            mms.append(led.emit("tensor", lambda e, _half=half, _ps=ps: e.matmul(
                _ps[:], s["ones_row"][:], s["b_all"][0:1, 512 * _half:512 * (_half + 1)],
                start=False, stop=True), deps=[s["ball_ready"], mms[-1]]))
            at_toks.append(mms[-1])
        tok_x = led.emit("sync", lambda e, _t=t, _s=slot: e.dma_start(
            out=d["xd"][_s][:], in_=x_ap[128 * _t:128 * (_t + 1), :]),
            deps=deps_c + [st.get(("xd_free", slot))], dma=True)
        radds = []
        for half in range(2):
            radds.append(led.emit("vector", lambda e, _s=slot, _half=half: e.tensor_tensor(
                d["x1t"][_s][:, 512 * _half:512 * (_half + 1)],
                d["xd"][_s][:, 512 * _half:512 * (_half + 1)],
                Pat[_half][:], ALU.add),
                deps=[at_toks[half], tok_x, st.get(("x1_free", slot))]))
            st[("at_free", half)] = radds[-1]
        st[("xd_free", slot)] = radds[-1]
        x1_toks.append(led.emit("sync", lambda e, _s=slot, _t=t: e.dma_start(
            out=x1_out[128 * _t:128 * (_t + 1), :], in_=d["x1t"][_s][:]),
            deps=radds, dma=True, out=True))
        if upto == "res":
            st[("x1_free", slot)] = x1_toks[-1]
            continue
        tln = emit_ln_tile(p, d, slot, radds + [x1_toks[-1]], d["ln2t"][slot][:],
                           in_ap=d["x1t"][slot])
        st[("x1_free", slot)] = tln
        ln2_toks.append(led.emit("gpsimd", lambda e, _s=slot, _t=t: e.dma_start(
            out=ln2_out[128 * _t:128 * (_t + 1), :], in_=d["ln2t"][_s][:]),
            deps=[tln], dma=True, out=True))
        if upto == "ln2":
            continue
        trs = []
        for jh in range(2):
            tts = []
            for j4 in range(4):
                j = 4 * jh + j4
                tts.append(led.emit("tensor", lambda e, _j=j, _j4=j4, _s=slot: e.transpose(
                    tp0[:, 128 * _j4:128 * (_j4 + 1)],
                    d["ln2t"][_s][:, 128 * _j:128 * (_j + 1)],
                    s["ident"][:]), deps=[tln, st.get("tp0_free")]))
            ev = led.emit("vector", lambda e, _jh=jh: e.tensor_copy(
                d["ln2T"][:, 512 * _jh:512 * (_jh + 1)], tp0[:, 0:512]), deps=tts)
            st["tp0_free"] = ev
            trs.append(ev)
        gmms = []
        for j in range(8):
            dep = (trs + [st.get("lg_free")]) if j == 0 else []
            gmms.append(led.emit("tensor", lambda e, _j=j: e.matmul(
                Plg[:, 0:E], d["ln2T"][:, 128 * _j:128 * (_j + 1)],
                s["gate_wT_sb"][:, E * _j:E * (_j + 1)],
                start=(_j == 0), stop=(_j == 7)), deps=dep))
        tlg = led.emit("vector", lambda e: e.tensor_copy(d["lgf"][:], Plg[:, 0:E]),
                       deps=[gmms[-1], st.get("lgf_free")])
        st["lg_free"] = tlg
        t1 = led.emit("vector", lambda e: e.reduce_max(d["lmax"][:, 0:1], d["lgf"][:], AX.X),
                      deps=[tlg])
        t2 = led.emit("vector", lambda e: e.tensor_scalar_mul(
            d["lmax"][:, 1:2], d["lmax"][:, 0:1], -1.0), deps=[t1])
        t3 = led.emit("scalar", lambda e: e.activation(
            d["scr16"][:], d["lgf"][:], AF.Exp, bias=d["lmax"][:, 1:2],
            accum_out=d["sume"][:]), deps=[t2, st.get("scr_free")])
        t4 = led.emit("vector", lambda e, _t=t: e.reciprocal(
            d["ptop_all"][:, _t:_t + 1], d["sume"][:]), deps=[t3])
        st["scr_free"] = t4
        t5 = led.emit("vector", lambda e: e.tensor_scalar(
            d["oh"][:], d["lgf"][:], d["lmax"][:, 0:1], None, ALU.is_equal),
            deps=[t1, st.get("oh_free")])
        t5b = led.emit("vector", lambda e, _t=t: e.tensor_copy(
            d["onehot_all"][:, E * _t:E * (_t + 1)], d["oh"][:]), deps=[t5])
        t6 = led.emit("vector", lambda e: e.tensor_copy(d["oh_bf"][:], d["oh"][:]),
                      deps=[t5, st.get("ohbf_free")])
        st["lgf_free"] = t5
        m1 = led.emit("tensor", lambda e: e.matmul(
            Ploc[:, 0:E], s["triu"][:], d["oh_bf"][:], start=True, stop=False),
            deps=[t6, st.get("loc_free")])
        m2 = led.emit("tensor", lambda e: e.matmul(
            Ploc[:, 0:E], s["ones_row_f32"][:], d["prefrow"][:],
            start=False, stop=True), deps=[m1, st["pref_ready"]])
        mc = led.emit("tensor", lambda e, _t=t: e.matmul(
            Pcnt[0:1, 0:E], s["ones_col"][:], d["oh_bf"][:],
            start=(_t == 0), stop=(_t == NTILE - 1)),
            deps=[t6, st.get("cnt_read")])
        st["ohbf_free"] = mc
        tpr = led.emit("vector", lambda e: e.tensor_copy(d["prefrow"][:], Pcnt[0:1, 0:E]),
                       deps=[mc, m2])
        st["pref_ready"] = tpr
        st["cnt_read"] = tpr
        r1 = led.emit("vector", lambda e: e.tensor_tensor(
            d["tmp16"][:], d["oh"][:], Ploc[:, 0:E], ALU.mult),
            deps=[m2, t5, st.get("tmp_free")])
        st["loc_free"] = r1
        r2 = led.emit("vector", lambda e, _t=t: e.reduce_sum(
            d["rank_all"][:, _t:_t + 1], d["tmp16"][:], AX.X), deps=[r1])
        r3 = led.emit("vector", lambda e: e.tensor_tensor(
            d["tmp16"][:], d["oh"][:], s["iota16"][:], ALU.mult),
            deps=[r2, t5])
        st["oh_free"] = r3
        r4 = led.emit("vector", lambda e, _t=t: e.reduce_sum(
            d["eidx_all"][:, _t:_t + 1], d["tmp16"][:], AX.X), deps=[r3])
        st["tmp_free"] = r4
    if upto == "full":
        tcnt = led.emit("vector", lambda e: e.tensor_copy(d["cntrow"][:], Pcnt[0:1, 0:E]),
                        deps=[st["cnt_read"]])
        d["cnt_token"] = tcnt
    else:
        d["cnt_token"] = None
    d["x1_tokens"] = x1_toks
    d["ln2_tokens"] = ln2_toks
    d["stageD_done"] = [led.barrier_token("vector"), led.barrier_token("scalar"),
                        led.barrier_token("sync_dma"), led.barrier_token("gpsimd_dma"),
                        led.barrier_token("out_dma")]
    return d

BIGF = float(1 << 23)
RW = 3104


def alloc_e_bufs(p, trans_scope=None):
    e = {}
    ts = trans_scope
    e["w2sb"] = p.sbuf("w2sb", [128, 32 * D], BF16, scope=ts)
    e["w1q"] = [p.sbuf(f"w1q{i}", [128, 8 * 1024], BF16, scope=ts) for i in range(4)]
    e["dispT"] = [p.sbuf(f"dispT{i}", [128, 8 * LC], BF16, scope=ts) for i in range(2)]
    e["htile"] = [p.sbuf(f"htile{i}", [128, 256], BF16, scope=ts) for i in range(2)]
    e["dtok"] = [p.sbuf(f"dtok{i}", [128, D], BF16, scope=ts) for i in range(2)]
    e["xps"] = [p.sbuf(f"xps{i}", [128, RW - 1024], BF16, scope=ts) for i in range(2)]
    e["o2st"] = [p.sbuf(f"o2st{i}", [128, D], F32, scope=ts) for i in range(2)]
    e["stg"] = [p.sbuf(f"stg{i}", [128, RW], BF16, scope=ts) for i in range(2)]
    e["counts_sb"] = p.sbuf("counts_sb", [8, E], F32, scope=ts)
    e["pmask"] = p.sbuf("pmask", [8, 1], F32, scope=ts)
    e["offrow"] = p.sbuf("offrow", [1, E], F32, scope=ts)
    e["offb"] = p.sbuf("offb", [128, E], F32, scope=ts)
    e["g1"] = p.sbuf("g1", [128, 10], F32, scope=ts)
    e["gi"] = p.sbuf("gi", [128, 2], I32, scope=ts)
    e["cslot"] = p.sbuf("cslot", [128, NTILE], I32, scope=ts)
    e["ohE"] = p.sbuf("ohE", [128, E], F32, scope=ts)
    e["tmpE"] = p.sbuf("tmpE", [128, E], F32, scope=ts)
    return e


def build_stage_e(p, s, d, e, dram, outslot_ap, cslot_ap, deps_d, core_inputs_loaded):
    nc, led = p.nc, p.led
    PF = p.ps["PF"]
    PH = [PsumSlice(PF, 0, 256), PsumSlice(PF, 512, 256)]
    PO = [PsumSlice(PF, 1024 + 512 * i) for i in range(4)]
    tp0 = p.ps["tp0"]
    st = {}
    dep0 = list(deps_d) + [core_inputs_loaded]

    m1 = led.emit("tensor", lambda e_: e_.matmul(
        PF[0:1, 0:E], e["pmask"][:], e["counts_sb"][:], start=True, stop=True),
        deps=dep0)
    t1 = led.emit("vector", lambda e_: e_.tensor_copy(e["offrow"][:], PF[0:1, 0:E]),
                  deps=[m1])
    m2 = led.emit("tensor", lambda e_: e_.matmul(
        PF[:, 0:E], s["ones_row_f32"][:], e["offrow"][:], start=True, stop=True),
        deps=[t1])
    t2 = led.emit("vector", lambda e_: e_.tensor_copy(e["offb"][:], PF[:, 0:E]),
                  deps=[m2])
    st["pf_free"] = t2

    tz = led.emit("vector", lambda e_: e_.memset(e["stg"][0][:], 0.0), deps=dep0)
    zf = []
    for r in range(E * LC // 128):
        zf.append(led.emit("scalar", lambda e_, _r=r: e_.dma_start(
            out=dram["disp"][128 * _r:128 * (_r + 1), :], in_=e["stg"][0][:]),
            deps=[tz], dma=True))
    zf_bar = led.barrier_token("scalar_dma")
    st[("stg_free", 0)] = zf[-1]

    scat_toks = []
    G = e["g1"]
    for t in range(NTILE):
        slot = t % 2
        o1 = led.emit("vector", lambda e_, _t=t: e_.tensor_scalar(
            e["ohE"][:], s["iota16"][:], d["eidx_all"][:, _t:_t + 1], None,
            ALU.is_equal), deps=dep0 + [st.get("ohE_free")])
        o2 = led.emit("vector", lambda e_: e_.tensor_tensor(
            e["tmpE"][:], e["ohE"][:], e["offb"][:], ALU.mult),
            deps=[o1, t2, st.get("tmpE_free")])
        st["ohE_free"] = o2
        o3 = led.emit("vector", lambda e_: e_.reduce_sum(
            G[:, 0:1], e["tmpE"][:], AX.X), deps=[o2, st.get("g_free")])
        st["tmpE_free"] = o3
        o4 = led.emit("vector", lambda e_, _t=t: e_.tensor_tensor(
            G[:, 1:2], G[:, 0:1], d["rank_all"][:, _t:_t + 1], ALU.add), deps=[o3])
        o5 = led.emit("vector", lambda e_: e_.tensor_scalar_add(
            G[:, 2:3], G[:, 1:2], -1.0), deps=[o4])
        o6 = led.emit("vector", lambda e_: e_.tensor_single_scalar(
            G[:, 3:4], G[:, 2:3], float(CAP), ALU.is_lt), deps=[o5])
        o7 = led.emit("vector", lambda e_, _t=t: e_.tensor_single_scalar(
            G[:, 4:5], d["rank_all"][:, _t:_t + 1], LC + 0.5, ALU.is_lt), deps=dep0 + [o6])
        o8 = led.emit("vector", lambda e_: e_.tensor_tensor(
            G[:, 5:6], G[:, 3:4], G[:, 4:5], ALU.mult), deps=[o7])
        o9 = led.emit("vector", lambda e_, _t=t: e_.scalar_tensor_tensor(
            G[:, 6:7], d["eidx_all"][:, _t:_t + 1], float(LC),
            d["rank_all"][:, _t:_t + 1], ALU.mult, ALU.add), deps=[o3, st.get("g_free")])
        o9b = led.emit("vector", lambda e_: e_.tensor_scalar_add(
            G[:, 6:7], G[:, 6:7], -1.0 - BIGF), deps=[o9])
        o11 = led.emit("vector", lambda e_: e_.tensor_tensor(
            G[:, 8:9], G[:, 6:7], G[:, 5:6], ALU.mult), deps=[o9b, o8])
        o12 = led.emit("vector", lambda e_: e_.tensor_scalar_add(
            G[:, 9:10], G[:, 8:9], BIGF), deps=[o11])
        o13 = led.emit("vector", lambda e_, _t=t: e_.tensor_copy(
            e["cslot"][:, _t:_t + 1], G[:, 9:10]), deps=[o12])
        st["g_free"] = o13
        stg = e["stg"][slot]
        c1 = led.emit("sync", lambda e_, _t=t, _stg=stg: e_.dma_start(
            out=_stg[:, 0:1024], in_=dram["ln2_d"][128 * _t:128 * (_t + 1), :]),
            deps=dep0 + [st.get(("stg_free", slot))], dma=True)
        c2 = led.emit("sync", lambda e_, _t=t, _stg=stg: e_.dma_start(
            out=_stg[:, 1024:3072].bitcast(F32),
            in_=dram["x1_d"][128 * _t:128 * (_t + 1), :]),
            deps=dep0 + [st.get(("stg_free", slot))], dma=True)
        c3 = led.emit("vector", lambda e_, _t=t, _stg=stg: e_.tensor_copy(
            _stg[:, 3072:3074].bitcast(F32), d["ptop_all"][:, _t:_t + 1]),
            deps=dep0 + [st.get(("stg_free", slot))])
        _gi_ap = e["gi"][:, 0:1]
        _cs_ap = e["cslot"][:, t:t + 1]
        tcv = led.emit("vector", lambda e_, _g=_gi_ap, _c=_cs_ap: e_.tensor_copy(
            _g, _c), deps=[o13, st.get("gi_free")])
        _stg_ap = stg[:]
        tsc = led.emit("gpsimd", lambda e_, _g=_gi_ap, _l=_stg_ap: e_.indirect_dma_start(
            out=dram["disp"],
            out_offset=IndirectOffsetOnAxis(_g, 0),
            in_=_l,
            in_offset=None, bounds_check=E * LC - 1, oob_is_err=False),
            deps=[tcv, c1, c2, c3, zf_bar], dma=True)
        st["gi_free"] = tsc
        st[("stg_free", slot)] = tsc
        scat_toks.append(tsc)
    tcs = led.emit("sync", lambda e_: e_.dma_start(out=cslot_ap, in_=e["cslot"][:]),
                   deps=[led.barrier_token("vector")], dma=True, out=True)

    lastfree = {}
    for ex in range(E):
        exslot = ex % 2
        dT = e["dispT"][exslot]
        w2tok = led.emit("gpsimd", lambda e_, _ex=ex: e_.dma_start(
            out=e["w2sb"][:].rearrange("p (c o) -> p c o", c=32),
            in_=dram["w2"][4096 * _ex:4096 * (_ex + 1), :].rearrange("(c p) o -> p c o", p=128)),
            deps=[lastfree.get("w2")], dma=True)
        w1toks = []
        for qq in range(4):
            tq = []
            for j in range(8):
                tq.append(led.emit("gpsimd", lambda e_, _ex=ex, _qq=qq, _j=j: e_.dma_start(
                    out=e["w1q"][_qq][:, 1024 * _j:1024 * (_j + 1)],
                    in_=dram["w1"][1024 * _ex + 128 * _j:1024 * _ex + 128 * (_j + 1),
                                   1024 * _qq:1024 * (_qq + 1)]),
                    deps=[lastfree.get(("w1", qq))], dma=True))
            w1toks.append(tq)
        trdone = []
        for k in range(LC // 128):
            row0 = LC * ex + 128 * k
            tg = led.emit("sync", lambda e_, _r=row0, _k=k: e_.dma_start(
                out=e["dtok"][_k % 2][:], in_=dram["disp"][_r:_r + 128, 0:1024]),
                deps=scat_toks + [lastfree.get(("dtok", k % 2))], dma=True)
            for jh in range(2):
                tts = []
                for j4 in range(4):
                    j = 4 * jh + j4
                    tts.append(led.emit("tensor", lambda e_, _j=j, _j4=j4, _k=k: e_.transpose(
                        tp0[:, 128 * _j4:128 * (_j4 + 1)],
                        e["dtok"][_k % 2][:, 128 * _j:128 * (_j + 1)],
                        s["ident"][:]), deps=[tg, st.get("tp0_free")]))
                ev = led.emit("vector", lambda e_, _jh=jh, _k=k, _dT=dT: e_.tensor_copy(
                    _dT.ap().rearrange("p (j c) -> p j c", j=8)[:, 4 * _jh:4 * (_jh + 1), 128 * _k:128 * (_k + 1)],
                    tp0[:, 0:512].rearrange("p (j c) -> p j c", j=4)),
                    deps=tts + [lastfree.get(("dT", exslot))])
                st["tp0_free"] = ev
                trdone.append(ev)
            lastfree[("dtok", k % 2)] = trdone[-1]
        ex_done = []
        blocks = [(0, 256), (256, 256), (512, 128)]
        for (b0, bw) in blocks:
            ns2 = bw // 128
            last_rl = {}
            for hc in range(32):
                qq, hh = hc // 8, hc % 8
                ph = PH[hc % 2]
                hs = e["htile"][hc % 2]
                mmq = []
                for j in range(8):
                    dep = []
                    if j == 0:
                        dep = list(w1toks[qq]) + trdone + [st.get(("ph_free", hc % 2))]
                    mmq.append(led.emit("tensor", lambda e_, _j=j, _hh=hh, _qq=qq, _b0=b0, _bw=bw, _ph=ph, _dT=dT: e_.matmul(
                        _ph[0:128, 0:_bw], e["w1q"][_qq][:, 1024 * _j + 128 * _hh:1024 * _j + 128 * (_hh + 1)],
                        _dT[:, LC * _j + _b0:LC * _j + _b0 + _bw],
                        start=(_j == 0), stop=(_j == 7)), deps=dep))
                rl = led.emit("scalar", lambda e_, _ph=ph, _bw=bw, _hs=hs: e_.activation(
                    _hs[:, 0:_bw], _ph[0:128, 0:_bw], AF.Relu),
                    deps=[mmq[-1], st.get(("hs_free", hc % 2))])
                mm2l = {}
                for s2 in range(ns2):
                    for oh in range(2):
                        i = 2 * s2 + oh
                        po = PO[i]
                        dep = [rl]
                        if hc == 0:
                            dep += [w2tok, st.get(("po_free", i))]
                        mm2l[i] = led.emit("tensor", lambda e_, _hc=hc, _s2=s2, _oh=oh, _po=po, _hs=hs: e_.matmul(
                            _po[:], _hs[:, 128 * _s2:128 * (_s2 + 1)],
                            e["w2sb"][:, 1024 * _hc + 512 * _oh:1024 * _hc + 512 * (_oh + 1)],
                            start=(_hc == 0), stop=(_hc == 31)), deps=dep)
                st[("hs_free", hc % 2)] = mm2l[max(mm2l)]
                st[("ph_free", hc % 2)] = rl
                last_rl = mm2l
            for s2 in range(ns2):
                row0 = LC * ex + b0 + 128 * s2
                xslot = s2 % 2
                txp = led.emit("sync", lambda e_, _r=row0, _x=xslot: e_.dma_start(
                    out=e["xps"][_x][:], in_=dram["disp"][_r:_r + 128, 1024:RW]),
                    deps=scat_toks + [st.get(("xps_free", xslot))], dma=True)
                o2 = e["o2st"][s2 % 2]
                sts = []
                for oh in range(2):
                    i = 2 * s2 + oh
                    sts.append(led.emit("vector", lambda e_, _i=i, _oh=oh, _x=xslot, _o2=o2: e_.scalar_tensor_tensor(
                        _o2[:, 512 * _oh:512 * (_oh + 1)],
                        PO[_i][:],
                        e["xps"][_x][:, 2048:2050].bitcast(F32),
                        e["xps"][_x][:, 1024 * _oh:1024 * (_oh + 1)].bitcast(F32),
                        ALU.mult, ALU.add),
                        deps=[last_rl[i], txp, st.get(("o2_free", s2 % 2))]))
                    st[("po_free", i)] = sts[-1]
                st[("xps_free", xslot)] = sts[-1]
                tdm = led.emit("sync", lambda e_, _r=row0, _o2=o2: e_.dma_start(
                    out=outslot_ap[_r:_r + 128, :], in_=_o2[:]),
                    deps=sts, dma=True, out=True)
                st[("o2_free", s2 % 2)] = tdm
                ex_done.append(tdm)
        lastfree[("dT", exslot)] = ex_done[-1]
        lastfree["w2"] = ex_done[-1]
        for qq in range(4):
            lastfree[("w1", qq)] = ex_done[-1]
    led.emit("sync", lambda e_: None, deps=[led.barrier_token("out_dma")], inc=False)
    return led.barrier_token("out_dma")

from concourse.bass_utils import run_bass_kernel_spmd


def _build_L1():
    nc = bass.Bass("TRN2", target_bir_lowering=False, debug=False)
    dram = {}

    def din(name, shape, dtype):
        dram[name] = nc.dram_tensor("d_" + name, list(shape), dtype,
                                    kind="ExternalInput").ap()

    din("m_cs", [128, T], BF16)
    din("triu", [128, 128], BF16)
    din("ident", [128, 128], BF16)
    din("iota16", [128, E], F32)
    din("ones_row", [1, 128], BF16)
    din("ones_row_f32", [1, 128], F32)
    din("RT", [32, 32], F32)
    din("cattn_wT", [D, 3072], F32)
    din("cattn_b_pt", [128, 24], F32)
    din("q2l_wT", [HD, L], F32)
    din("q2l_w", [L, HD], F32)
    din("q2l_b", [L, 1], F32)
    din("kv2l_wT", [HD, L], F32)
    din("kv2l_w", [L, HD], F32)
    din("kv2l_b", [L, 1], F32)
    din("wqk_wT", [L, L], F32)
    din("wqk_b", [L, 1], F32)
    din("l2o_w", [HD, L], F32)
    din("l2o_wT", [L, HD], F32)
    din("l2o_b_row", [1, HD], F32)
    din("cprojT", [D, D], F32)
    din("cproj_b_row", [1, D], F32)
    din("gate_wT", [D, E], F32)
    din("x", [T, D], F32)
    o_x1 = nc.dram_tensor("o_x1", [T, D], F32, kind="ExternalOutput").ap()
    o_ln2 = nc.dram_tensor("o_ln2", [T, D], BF16, kind="ExternalOutput").ap()
    o_cnt = nc.dram_tensor("o_cnt", [1, E], F32, kind="ExternalOutput").ap()
    o_rk = nc.dram_tensor("o_rk", [128, NTILE], F32, kind="ExternalOutput").ap()
    o_ei = nc.dram_tensor("o_ei", [128, NTILE], F32, kind="ExternalOutput").ap()
    o_pt = nc.dram_tensor("o_pt", [128, NTILE], F32, kind="ExternalOutput").ap()

    with ExitStack() as ctx:
        led = Ledger(nc)
        led.setup_sems(ctx)
        p = Prog(nc, led, ctx)
        p.dram = dram
        tmp = ctx.enter_context(ExitStack())
        s = build_setup(p, dram, tmp)
        tmp.close()
        alloc_psums(p)
        dpersist = alloc_d_persist(p)
        attn_scope = ExitStack()
        atrans = ExitStack()
        b = alloc_attn_bufs(p, pscope=attn_scope, trans_scope=atrans)
        build_stage_a(p, s, b, dram["x"], war_deps=s["done"])
        atrans.close()
        ctrans = ExitStack()
        cb = alloc_c_bufs(p, pscope=attn_scope, trans_scope=ctrans)
        build_stage_c(p, s, b, cb, deps_a=b["stageA_done"])
        ctrans.close()
        dtrans = ExitStack()
        alloc_d_bufs(p, dpersist, trans_scope=dtrans)
        build_stage_d(p, s, cb, dpersist, dram["x"], o_x1, o_ln2,
                      deps_c=cb["stageC_done"])
        dtrans.close()
        attn_scope.close()
        done = dpersist["stageD_done"] + [dpersist["cnt_token"]]
        led.emit("gpsimd", lambda e: e.dma_start(out=o_cnt, in_=dpersist["cntrow"][:]),
                 deps=done, dma=True, out=True)
        led.emit("gpsimd", lambda e: e.dma_start(out=o_rk, in_=dpersist["rank_all"][:]),
                 deps=done, dma=True, out=True)
        led.emit("gpsimd", lambda e: e.dma_start(out=o_ei, in_=dpersist["eidx_all"][:]),
                 deps=done, dma=True, out=True)
        led.emit("gpsimd", lambda e: e.dma_start(out=o_pt, in_=dpersist["ptop_all"][:]),
                 deps=done, dma=True, out=True)
        led.emit("gpsimd", lambda e: None, deps=[led.barrier_token("out_dma")], inc=False)
        block = ctx.enter_context(nc.Block())
        led.play(block)
    return nc


def _build_L2():
    nc = bass.Bass("TRN2", target_bir_lowering=False, debug=False)
    dram = {}

    def din(name, shape, dtype):
        dram[name] = nc.dram_tensor("d_" + name, list(shape), dtype,
                                    kind="ExternalInput").ap()

    din("iota16", [128, E], F32)
    din("ident", [128, 128], BF16)
    din("ones_row_f32", [1, 128], F32)
    din("counts_all", [NCORE, E], F32)
    din("pmask", [NCORE, 1], F32)
    din("rank", [128, NTILE], F32)
    din("eidx", [128, NTILE], F32)
    din("ptop", [128, NTILE], F32)
    din("ln2", [T, D], BF16)
    din("x1", [T, D], F32)
    din("w1", [E * D, 4096], F32)
    din("w2", [E * 4096, D], F32)
    dram["ln2_d"] = dram["ln2"]
    dram["x1_d"] = dram["x1"]
    dram["disp"] = nc.dram_tensor("disp", [E * LC, RW], BF16).ap()
    o_slot = nc.dram_tensor("o_slot", [E * LC, D], F32, kind="ExternalOutput").ap()
    o_cs = nc.dram_tensor("o_cs", [128, NTILE], I32, kind="ExternalOutput").ap()

    with ExitStack() as ctx:
        led = Ledger(nc)
        led.setup_sems(ctx)
        p = Prog(nc, led, ctx)
        p.dram = dram
        alloc_psums(p)
        s = {}
        s["iota16"] = p.sbuf("iota16", [128, E], F32)
        s["ident"] = p.sbuf("ident", [128, 128], BF16)
        s["ones_row_f32"] = p.sbuf("ones_row_f32", [1, 128], F32)
        d = {}
        d["rank_all"] = p.sbuf("rank_all", [128, NTILE], F32)
        d["eidx_all"] = p.sbuf("eidx_all", [128, NTILE], F32)
        d["ptop_all"] = p.sbuf("ptop_all", [128, NTILE], F32)
        etrans = ctx.enter_context(ExitStack())
        eb = alloc_e_bufs(p, trans_scope=etrans)
        toks = []
        for nm, dst in [("iota16", s["iota16"]), ("ident", s["ident"]),
                        ("ones_row_f32", s["ones_row_f32"]),
                        ("rank", d["rank_all"]), ("eidx", d["eidx_all"]),
                        ("ptop", d["ptop_all"]),
                        ("counts_all", eb["counts_sb"]), ("pmask", eb["pmask"])]:
            toks.append(led.emit("sync", lambda e, _n=nm, _d=dst: e.dma_start(
                out=_d[:], in_=dram[_n]), dma=True))
        cil = led.emit("vector", lambda e: e.tensor_copy(
            eb["g1"][0:1, 0:1], eb["counts_sb"][0:1, 0:1]), deps=toks)
        build_stage_e(p, s, d, eb, dram, o_slot, o_cs, deps_d=[],
                      core_inputs_loaded=cil)
        etrans.close()
        block = ctx.enter_context(nc.Block())
        led.play(block)
    return nc


def kernel(**inputs):
    consts = host_constants()
    w = host_weight_layouts(inputs)
    cattn_b_pt = np.ascontiguousarray(
        inputs["L0_attn_c_attn_b"].reshape(24, 128).T)
    x = inputs["L0_attn_x"]

    base = {
        "d_m_cs": consts["m_cs"], "d_triu": consts["triu128"],
        "d_ident": consts["ident128"], "d_iota16": consts["iota16"],
        "d_ones_row": consts["ones_row"], "d_ones_row_f32": consts["ones_row_f32"],
        "d_RT": consts["RT"], "d_cattn_wT": w["cattn_wT"],
        "d_cattn_b_pt": cattn_b_pt, "d_q2l_wT": w["q2l_wT"], "d_q2l_w": w["q2l_w"],
        "d_q2l_b": w["q2l_b"], "d_kv2l_wT": w["kv2l_wT"], "d_kv2l_w": w["kv2l_w"],
        "d_kv2l_b": w["kv2l_b"], "d_wqk_wT": w["wqk_wT"], "d_wqk_b": w["wqk_b"],
        "d_l2o_w": w["l2o_w"], "d_l2o_wT": w["l2o_wT"], "d_l2o_b_row": w["l2o_b_row"],
        "d_cprojT": w["cprojT"], "d_cproj_b_row": w["cproj_b_row"],
        "d_gate_wT": w["gate_wT"],
    }
    nc1 = _build_L1()
    in_maps = [dict(base, d_x=np.ascontiguousarray(x[c])) for c in range(NCORE)]
    r1 = run_bass_kernel_spmd(nc1, in_maps, core_ids=list(range(NCORE))).results

    counts_all = np.concatenate([r1[c]["o_cnt"] for c in range(NCORE)], axis=0)
    w1r = np.ascontiguousarray(inputs["L0_ffn_w1"].reshape(E * D, 4096))
    w2r = np.ascontiguousarray(inputs["L0_ffn_w2"].reshape(E * 4096, D))
    nc2 = _build_L2()
    in_maps2 = []
    for c in range(NCORE):
        pm = np.zeros((NCORE, 1), np.float32)
        pm[:c, 0] = 1.0
        in_maps2.append({
            "d_iota16": consts["iota16"], "d_ident": consts["ident128"],
            "d_ones_row_f32": consts["ones_row_f32"],
            "d_counts_all": counts_all, "d_pmask": pm,
            "d_rank": r1[c]["o_rk"], "d_eidx": r1[c]["o_ei"],
            "d_ptop": r1[c]["o_pt"], "d_ln2": r1[c]["o_ln2"],
            "d_x1": r1[c]["o_x1"], "d_w1": w1r, "d_w2": w2r,
        })
    r2 = run_bass_kernel_spmd(nc2, in_maps2, core_ids=list(range(NCORE))).results

    out = np.empty((NCORE, T, D), np.float32)
    for c in range(NCORE):
        cs = np.empty(T, np.int64)
        ocs = r2[c]["o_cs"]
        for t in range(NTILE):
            cs[t * 128:(t + 1) * 128] = ocs[:, t]
        kept = cs < E * LC
        oc = r1[c]["o_x1"].copy()
        oc[kept] = r2[c]["o_slot"][cs[kept]]
        out[c] = oc
    return out
